# version 01 (final)
# baseline (speedup 1.0000x reference)
import numpy as np
import ml_dtypes
import concourse.bass as bass
import concourse.mybir as mybir
from concourse.bass_utils import run_bass_kernel_spmd

F32 = mybir.dt.float32
F32R = mybir.dt.float32r
BF16 = mybir.dt.bfloat16
AF = mybir.ActivationFunctionType
OP = mybir.AluOpType
AX = mybir.AxisListType

D = 2048
TB = 512
NBLK = 3
NSUB = TB // 16
NTOK = 1 + NBLK * TB
NSLAB = 43
EPS = 1e-5

ENGINES = ("tensor", "vector", "scalar", "gpsimd", "sync")
_ESIZE = {F32: 4, F32R: 4, BF16: 2}

_CACHE = {}
TRUNC = 5
FINE = 99
FINE3 = 99
DEBUG = 0


class _EngProxy:

    def __init__(self, E, sched, eng):
        object.__setattr__(self, "_E", E)
        object.__setattr__(self, "_S", sched)
        object.__setattr__(self, "_eng", eng)
        object.__setattr__(self, "last", None)

    def sync(self):
        if self.last is None:
            return
        S, eng = self._S, self._eng
        S.auxcnt[eng] += 1
        self.last.then_inc(S.auxsem[eng], 1)
        self._E.wait_ge(S.auxsem[eng], S.auxcnt[eng])
        object.__setattr__(self, "last", None)

    def __getattr__(self, n):
        f = getattr(self._E, n)
        if callable(f):
            def wrap(*a, **k):
                r = f(*a, **k)
                if r is not None and hasattr(r, "then_inc"):
                    object.__setattr__(self, "last", r)
                return r
            return wrap
        return f


class Sched:

    def __init__(self, nc, dma_counts=None):
        self.nc = nc
        self.q = {e: [] for e in ENGINES}
        self.nstep = {e: 0 for e in ENGINES}
        self.dma = {"syncq": 0, "gq": 0}
        self.dma_ord = {"syncq": [], "gq": []}
        self.dma_cum = {"syncq": 0, "gq": 0}
        self.dma_counts = dma_counts
        self.dma_meas = {"syncq": [], "gq": []}
        self.semobj = {}
        self.auxsem = {}
        self.auxcnt = {e: 0 for e in ENGINES}
        self.last_w = {}
        self.readers = {}
        self.last_w_eng = {}
        self.mark = {}

    def step(self, eng, body, reads=(), writes=(), extra=(), dma_q=None, n_dma=0):
        rtok = set()
        for t in reads:
            rtok.update(t)
        wtok = set()
        for t in writes:
            wtok.update(t)
        deps = list(extra)
        for b in rtok:
            if b in self.last_w:
                deps.append(self.last_w[b])
        for b in wtok:
            if b in self.last_w:
                if not (eng == "tensor" and isinstance(b, str)
                        and b.startswith("PB")
                        and self.last_w_eng.get(b) == "tensor"):
                    deps.append(self.last_w[b])
            for h in self.readers.get(b, ()):
                deps.append(h)
        if dma_q is None:
            self.nstep[eng] += 1
            handle = (eng, self.nstep[eng])
            myinc = eng
        else:
            ordinal = len(self.dma_ord[dma_q])
            cum_before = self.dma_cum[dma_q]
            if self.dma_counts is not None:
                self.dma_cum[dma_q] += 16 * self.dma_counts[dma_q][ordinal]
            else:
                self.dma_cum[dma_q] += 16
            self.dma_ord[dma_q].append(self.dma_cum[dma_q])
            handle = (dma_q, self.dma_ord[dma_q][ordinal])
            myinc = None
            deps = list(deps) + [(dma_q, cum_before)]

        waits = []
        for (skey, val) in deps:
            k = (eng, skey)
            if self.mark.get(k, 0) >= val:
                continue
            self.mark[k] = val
            waits.append((skey, val))

        def run(E, waits=waits, body=body, myinc=myinc, eng=eng, dma_q=dma_q):
            for (skey, val) in waits:
                E.wait_ge(self.semobj[skey], val)
            if myinc is None:
                b0 = self.nc.n_instructions()
                body(E)
                nd = self.nc.n_instructions() - b0
                self.dma_meas[dma_q].append(nd)
                return
            P = _EngProxy(E, self, eng)
            body(P)
            if P.last is not None:
                P.last.then_inc(self.semobj[myinc], 1)
            else:
                E.sem_inc(self.semobj[myinc], 1)

        self.q[eng].append(run)
        for b in rtok:
            self.readers.setdefault(b, []).append(handle)
        for b in wtok:
            self.last_w[b] = handle
            self.last_w_eng[b] = eng if dma_q is None else "dma"
            self.readers[b] = []
        return handle


def _tile_pf(w, pchunk=128):
    n = w.shape[0] // pchunk
    return np.ascontiguousarray(
        w.reshape(n, pchunk, -1).transpose(1, 0, 2).reshape(pchunk, -1))


def _host_prep(inputs):
    f32 = np.float32
    g = lambda k: np.ascontiguousarray(np.asarray(inputs[k], dtype=f32))
    H = {}
    x = g("L0_attn_x")
    mu_x, Wx1, Wx2 = g("L0_attn_mu_x"), g("L0_attn_Wx1"), g("L0_attn_Wx2")
    x_bias = g("L0_attn_x_bias")
    Wr, Wk, Wv = g("L0_attn_Wr"), g("L0_attn_Wk"), g("L0_attn_Wv")
    Ww_a, Ww_b, bw = g("L0_attn_Ww_a"), g("L0_attn_Ww_b"), g("L0_attn_bw")
    Wg_a, Wg_b, bg = g("L0_attn_Wg_a"), g("L0_attn_Wg_b"), g("L0_attn_bg")
    bonus = g("L0_attn_bonus")
    ln_w, ln_b = g("L0_attn_ln_w"), g("L0_attn_ln_b")
    Wo = g("L0_attn_Wo")
    lut = g("L0_ffn_lut")
    gidx = np.asarray(inputs["L0_ffn_gate_idx"])
    uidx = np.asarray(inputs["L0_ffn_up_idx"])
    didx = np.asarray(inputs["L0_ffn_down_idx"])

    H["xs"] = []
    for c in range(8):
        b, q = c // 4, c % 4
        T0 = q * 1024
        lo = T0 - (NTOK - 1024)
        xs = np.zeros((NTOK, D), dtype=f32)
        lo_c = max(lo, 0)
        xs[lo_c - lo:] = x[b, lo_c:T0 + 1024]
        H["xs"].append(np.ascontiguousarray(xs))

    H["mu_xf"] = np.ascontiguousarray(mu_x.reshape(16, 128).T)
    H["xbias_f"] = np.ascontiguousarray(
        np.stack([x_bias[n].reshape(16, 128).T for n in range(5)], 1).reshape(128, 80))
    H["wx1"] = _tile_pf(Wx1)
    H["wx2"] = np.ascontiguousarray(Wx2.transpose(1, 2, 0))
    H["wr"] = np.ascontiguousarray(
        Wr.reshape(16, 128, 2, 512).transpose(1, 2, 0, 3).reshape(128, -1))
    H["wk"] = np.ascontiguousarray(
        Wk.reshape(16, 128, 2, 512).transpose(1, 2, 0, 3).reshape(128, -1))
    H["wv"] = _tile_pf(Wv)
    H["wwa"] = _tile_pf(Ww_a)
    H["wwb"] = np.ascontiguousarray(Ww_b)
    H["bwf"] = np.ascontiguousarray(bw.reshape(8, 128).T)
    H["wga"] = _tile_pf(Wg_a)
    H["wgb"] = np.ascontiguousarray(np.concatenate([Wg_b, bg[None, :]], 0))
    H["bonus_f"] = np.ascontiguousarray(bonus.reshape(1024).reshape(8, 128).T)
    H["lnw_x"] = np.ascontiguousarray(np.broadcast_to(np.tile(ln_w, 16), (128, 1024)))
    H["lnb_x"] = np.ascontiguousarray(np.broadcast_to(np.tile(ln_b, 16), (128, 1024)))
    H["wo"] = _tile_pf(Wo)
    WgT = lut[gidx].T
    WuT = lut[uidx].T
    WdT = lut[didx].T
    H["wgf"] = np.ascontiguousarray(
        WgT.reshape(16, 128, 43, 128).transpose(1, 2, 0, 3).reshape(128, -1))
    H["wuf"] = np.ascontiguousarray(
        WuT.reshape(16, 128, 43, 128).transpose(1, 2, 0, 3).reshape(128, -1))
    H["wdf"] = _tile_pf(WdT)
    H["iden"] = np.eye(128, dtype=f32)
    u = np.arange(128)
    cm = np.zeros((128, 256), f32)
    cm[:, 0:128] = ((u[:, None] // 16 == u[None, :] // 16)
                    & (u[:, None] < u[None, :])).astype(f32)
    cm[:, 128:256] = np.eye(128, dtype=f32)
    H["cmask_x"] = np.ascontiguousarray(cm)
    return H


def build_program(dma_counts=None):
    nc = bass.Bass("TRN2", target_bir_lowering=False)

    def din(name, shape, dtype=F32R):
        return nc.dram_tensor(name, list(shape), dtype, kind="ExternalInput").ap()

    xs_d = din("xs", [NTOK, D], F32)
    mu_xf_d = din("mu_xf", [128, 16], F32)
    xbias_d = din("xbias_f", [128, 80], F32)
    wx1_d = din("wx1", [128, 16 * 320])
    wx2_d = din("wx2", [5, 64, 2048])
    wr_d = din("wr", [128, 16384])
    wk_d = din("wk", [128, 16384])
    wv_d = din("wv", [128, 16384])
    wwa_d = din("wwa", [128, 1024])
    wwb_d = din("wwb", [64, 1024])
    bwf_d = din("bwf", [128, 8], F32)
    wga_d = din("wga", [128, 1024])
    wgb_d = din("wgb", [65, 1024])
    bonus_d = din("bonus_f", [128, 8], F32)
    lnw_d = din("lnw_x", [128, 1024], F32)
    lnb_d = din("lnb_x", [128, 1024], F32)
    wo_d = din("wo", [128, 16384])
    wgf_d = din("wgf", [128, 43 * 2048])
    wuf_d = din("wuf", [128, 43 * 2048])
    wdf_d = din("wdf", [128, 43 * 2048], F32)
    iden_d = din("iden", [128, 128], F32)
    cmask_d = din("cmask_x", [128, 256], F32)

    y_d = nc.dram_tensor("y", [1024, D], F32, kind="ExternalOutput").ap()
    x1_hbm = nc.dram_tensor("x1spill", [1024, D], F32,
                            kind="ExternalOutput" if DEBUG else "Internal").ap()
    if DEBUG:
        odbg_d = nc.dram_tensor("odbg", [2, 128, 4096], F32,
                                kind="ExternalOutput").ap()
    if DEBUG >= 2:
        dbgk_d = nc.dram_tensor("dbgk", [4, 2, 128, 2048], BF16,
                                kind="ExternalOutput").ap()
        dbgs_d = nc.dram_tensor("dbgs", [4, 128, 512], F32,
                                kind="ExternalOutput").ap()
    if DEBUG >= 3:
        dbgc_d = nc.dram_tensor("dbgc", [128, 8 * 513], F32,
                                kind="ExternalOutput").ap()
    if DEBUG >= 4:
        dbgw_d = nc.dram_tensor("dbgw", [3, 128, 1024], F32,
                                kind="ExternalOutput").ap()
    if DEBUG >= 5:
        dbgx_d = nc.dram_tensor("dbgx", [128, 2048], F32,
                                kind="ExternalOutput").ap()
        dbgxi_d = nc.dram_tensor("dbgxi", [128, 2048], F32,
                                 kind="ExternalOutput").ap()
        dbgxn_d = nc.dram_tensor("dbgxn", [128, 8192], F32,
                                 kind="ExternalOutput").ap()
    h_hbm = nc.dram_tensor("hspill", [128, NSLAB * 1024], BF16, kind="Internal").ap()
    xn2_hbm = nc.dram_tensor("xn2spill", [128, 2 * 16 * TB], F32R, kind="Internal").ap()
    halo_hbm = nc.dram_tensor("halospill", [2048], F32, kind="Internal").ap()

    BASE = (nc.sbuf_base + 255) // 256 * 256
    TOP = nc.sbuf_top
    offs = {}
    cur = [BASE]

    def at(name, shape, dtype, offset=None, align=256):
        es = _ESIZE[dtype]
        n = 1
        for s in shape[1:]:
            n *= s
        nb = n * es
        if offset is None:
            off = (cur[0] + align - 1) // align * align
            cur[0] = off + nb
        else:
            off = offset
        assert off + nb <= TOP, (name, off, nb, TOP)
        offs[name] = (off, nb)
        return nc.alloc_sbuf_tensor_at(name, list(shape), dtype, offset=off).ap()

    GR = 1024

    def M(name, lo=0, ln=None):
        off, nb = offs[name]
        if ln is None:
            ln = nb - lo
        a = (off + lo) // GR
        b = (off + lo + ln + GR - 1) // GR
        return tuple(range(a, b))

    idn = at("idn", [128, 128], F32)
    cmask = at("cmask", [128, 256], F32)
    hsc = at("hsc", [16, 128], F32)
    mu_xf = at("mu_xf", [128, 16], F32)
    xbias = at("xbias", [128, 80], F32)
    bwf = at("bwf", [128, 8], F32)
    bonus_f = at("bonus_sb", [128, 8], F32)
    consts = at("consts", [128, 8], F32)
    stats = at("stats", [128, 48], F32)
    gnst = at("gnst", [128, 80], F32)
    xnh = at("xnh", [128, 16], F32)
    p16 = at("p16", [128, 8, NSUB], F32)
    S_sb = at("S_sb", [128, 512], F32)
    S_bf = at("S_bf", [128, 512], BF16, offset=offs["S_sb"][0] + 2048)
    cur[0] = offs["S_bf"][0] + 1024
    tlow = at("tlow", [65, 512], F32R)
    wwbb = at("wwbb", [65, 1024], F32R)
    lr = at("lr", [128, 3, TB], F32R)
    eWb = at("eWb", [128, TB], F32)
    v_bf = at("v_bf", [128, 4, 1024], BF16, offset=offs["lr"][0])
    ones = at("ones", [128, TB], F32)
    lnw = at("lnw", [128, 1024], F32)
    lnb = at("lnb", [128, 1024], F32)
    P0_END = cur[0]

    Z1 = (P0_END + 255) // 256 * 256
    x_tm = at("x_tm", [128, 4, D], F32, offset=Z1)
    r_fm = at("r_fm", [128, 8, TB], F32, offset=Z1)
    r_bf = at("r_bf", [128, 8, TB], BF16, offset=Z1)
    k_fm = at("k_fm", [128, 8, TB], F32, offset=Z1 + 16384)
    k_bf = at("k_bf", [128, 8, TB], BF16, offset=Z1 + 16384)
    wgbuf = at("wgbuf", [128, 2, 2048], F32R, offset=Z1)
    wubuf = at("wubuf", [128, 2, 2048], F32R, offset=Z1 + 16384)
    Z2 = Z1 + 32768
    xn_fm = at("xn_fm", [128, 16, TB], F32, offset=Z2)
    xn2f = at("xn2f", [128, 16, TB], F32R, offset=Z2)
    khtm2 = at("khtm2", [16, 2, 8, 1024], BF16, offset=Z2)
    xn2sb = at("xn2sb", [128, 16, 1024], F32R, offset=Z2)
    Z3 = Z2 + 32768
    delta1 = at("delta1", [128, 8, TB], F32, offset=Z3)
    delta2 = at("delta2", [128, 8, TB], F32, offset=Z3 + 16384)
    argA = at("argA", [128, 8, TB], F32, offset=Z3)
    argB = at("argB", [128, 8, TB], F32, offset=Z3 + 16384)
    v2bf = at("v2bf", [16, 2, 8, 1024], BF16, offset=Z3)
    Z4 = Z3 + 32768
    xi = at("xi", [128, 16, TB], F32, offset=Z4)
    xir = at("xir", [128, 16, TB], F32R, offset=Z4)
    khat = at("khat", [128, 8, TB], F32, offset=Z4)
    sqd = at("sqd", [128, 1024], F32, offset=Z3)
    rbt = at("rbt", [128, 8, TB], BF16, offset=Z4 + 16384)
    o2_fm = at("o2_fm", [128, 8, TB], F32R, offset=Z4)
    sqscr = at("sqscr", [128, 1024], F32, offset=Z4 + 16384)
    wdbuf = at("wdbuf", [128, 2, 512], F32, offset=Z4 + 24576)
    wdbf = at("wdbf", [128, 2, 512], BF16, offset=Z4 + 24576 + 4096)
    ksc = at("ksc", [128, 16, 128], BF16, offset=Z4 + 24576)
    msc = at("msc", [128, 4, 256], BF16, offset=Z4 + 28672)
    hbuf = at("hbuf", [128, NSLAB, 1024], BF16, offset=Z2)
    Z5 = Z4 + 32768
    cumW = at("cumW", [128, 8, TB + 1], F32, offset=Z5)
    o_tm = at("o_tm", [128, 4, 1024], F32, offset=Z5)
    x1st = at("x1st", [128, 8, 512], F32, offset=Z5)
    Z6 = Z5 + 8 * (TB + 1) * 4
    sg_bf = at("sg_bf", [128, 4, 1024], BF16, offset=Z6)
    Z7 = (Z6 + 4 * 1024 * 2 + 255) // 256 * 256
    wx1b = at("wx1b", [128, 8, 320], F32R, offset=Z7)
    wbuf = at("wbuf", [128, 2, 512], F32R, offset=Z7)
    wwab = at("wwab", [128, 1024], F32R, offset=Z7 + 4096)
    wx2b = at("wx2b", [128, 2048], F32R, offset=Z7 + 10240)
    khst = at("khst", [128, 1024], BF16, offset=Z7 + 10240)
    rsc = at("rsc", [128, 16, 128], BF16, offset=Z7 + 12288)
    wobuf = at("wobuf", [128, 2, 2048], F32R, offset=Z7)
    hstage = at("hstage", [128, 2, 1024], BF16, offset=Z7)
    ystage = at("ystage", [128, 2, 512], F32, offset=Z7 + 4096)
    ostage = at("ostage", [16, 2, 1024], F32, offset=Z7)
    Z7_END = Z7 + 18432
    assert Z7_END <= TOP, (Z7_END, TOP)

    import contextlib
    with contextlib.ExitStack() as ctx:
        psum = [ctx.enter_context(nc.psum_tensor(f"pb{i}", [128, 512], F32))
                for i in range(8)]
        S = Sched(nc, dma_counts)
        for e in ("tensor", "vector", "scalar", "gpsimd"):
            S.semobj[e] = ctx.enter_context(nc.semaphore("sem_" + e))
            S.auxsem[e] = ctx.enter_context(nc.semaphore("aux_" + e))
        S.semobj["syncq"] = ctx.enter_context(nc.semaphore("sem_syncq"))
        S.semobj["gq"] = ctx.enter_context(nc.semaphore("sem_gq"))

        PB = [p[:, :] for p in psum]

        def PT(i):
            return ("PB%d" % i,)

        def HT(name):
            return ("H_" + name,)

        def sync_load(dst_ap, src_ap, reads=(), writes=()):
            def body(E, d=dst_ap, s=src_ap):
                E.dma_start(out=d, in_=s).then_inc(S.semobj["syncq"], 16)
            return S.step("sync", body, reads=reads, writes=writes,
                          dma_q="syncq", n_dma=1)

        def g_dma(dst_ap, src_ap, reads=(), writes=()):
            def body(E, d=dst_ap, s=src_ap):
                E.dma_start(out=d, in_=s).then_inc(S.semobj["gq"], 16)
            return S.step("gpsimd", body, reads=reads, writes=writes,
                          dma_q="gq", n_dma=1)

        def b_init(E):
            E.memset(ones[:, :], 1.0)
            E.memset(consts[:, 0:1], EPS)
            E.memset(S_sb[:, :], 0.0)
            E.memset(S_bf[:, :], 0.0)
            E.memset(cumW[:, :, 0:1], 0.0)
            E.memset(xnh[:, :], 0.0)
        S.step("gpsimd", b_init,
               writes=(M("ones"), M("consts"), M("S_sb"), M("S_bf"),
                       M("cumW"), M("xnh")))
        S.step("scalar", lambda E: E.activation(out=tlow[64:65, :], in_=ones[0:1, :],
                                                func=AF.Identity),
               reads=(M("ones"),), writes=(M("tlow"),))

        for dst, src, nm in ((idn, iden_d, "idn"), (cmask, cmask_d, "cmask"),
                             (mu_xf, mu_xf_d, "mu_xf"),
                             (xbias, xbias_d, "xbias"), (bwf, bwf_d, "bwf"),
                             (bonus_f, bonus_d, "bonus_sb"), (lnw, lnw_d, "lnw"),
                             (lnb, lnb_d, "lnb")):
            sync_load(dst[:, :], src[:, :], writes=(M(nm),))

        cw_flat = cumW[:, :, :].rearrange("p a c -> p (a c)")
        halo_x = cw_flat[0:1, 0:D]
        halo_sq = cw_flat[0:1, 2048:2048 + D]

        BR = {"r": 0, "w": 1, "k": 2, "v": 3, "g": 4}

        for b in range(min(NBLK, TRUNC - 1)):
            is_main = b > 0
            lo = b * TB

            h_halo = g_dma(halo_x, xs_d[lo:lo + 1, :], reads=(M("cumW"),),
                           writes=(M("cumW", 0, 8192),))
            hx_t = []
            for j in range(4):
                hx_t.append(g_dma(x_tm[:, j, :],
                                  xs_d[lo + 1 + j * 128:lo + 1 + (j + 1) * 128, :],
                                  reads=(M("x_tm", j * 8192, 8192),),
                                  writes=(M("x_tm", j * 8192, 8192),)))

            def emit_ln(src_tile_f, dst_xi, j, extra_dep):
                sb = 12 * j
                stt = (M("stats", 0, 192),)

                def b_sum(E, j=j, sb=sb):
                    E.tensor_reduce(stats[:, sb:sb + 1], src_tile_f(j), axis=AX.X,
                                    op=OP.add)
                    E.sync()
                    E.tensor_scalar(stats[:, sb + 2:sb + 3], stats[:, sb:sb + 1],
                                    1.0 / D, None, OP.mult, OP.bypass)
                S.step("vector", b_sum, reads=(M("x_tm", j * 8192, 8192),),
                       writes=stt, extra=extra_dep)

                def b_sq(E, j=j, sb=sb):
                    E.activation(out=sqd[:, :], in_=src_tile_f(j)[:, 0:1024],
                                 func=AF.Square, accum_out=stats[:, sb + 1:sb + 2])
                    E.sync()
                    E.activation(out=sqd[:, :], in_=src_tile_f(j)[:, 1024:2048],
                                 func=AF.Square, accum_out=stats[:, sb + 3:sb + 4])
                S.step("scalar", b_sq, reads=(M("x_tm", j * 8192, 8192),),
                       writes=stt + (M("sqd"),), extra=extra_dep)

                def b_var(E, j=j, sb=sb):
                    E.tensor_tensor(stats[:, sb + 1:sb + 2], stats[:, sb + 1:sb + 2],
                                    stats[:, sb + 3:sb + 4], OP.add)
                    E.sync()
                    E.tensor_scalar(stats[:, sb + 3:sb + 4], stats[:, sb + 1:sb + 2],
                                    1.0 / D, None, OP.mult, OP.bypass)
                    E.tensor_tensor(stats[:, sb + 4:sb + 5], stats[:, sb + 2:sb + 3],
                                    stats[:, sb + 2:sb + 3], OP.mult)
                    E.sync()
                    E.tensor_tensor(stats[:, sb + 5:sb + 6], stats[:, sb + 3:sb + 4],
                                    stats[:, sb + 4:sb + 5], OP.subtract)
                S.step("vector", b_var, reads=stt, writes=stt)
                S.step("scalar", lambda E, sb=sb: E.activation(
                    out=stats[:, sb + 6:sb + 7], in_=stats[:, sb + 5:sb + 6],
                    func=AF.Sqrt, bias=consts[:, 0:1]),
                    reads=stt + (M("consts"),), writes=stt)

                def b_rstd(E, sb=sb):
                    E.reciprocal(stats[:, sb + 7:sb + 8], stats[:, sb + 6:sb + 7])
                    E.sync()
                    E.scalar_tensor_tensor(stats[:, sb + 8:sb + 9],
                                           stats[:, sb + 2:sb + 3], -1.0,
                                           stats[:, sb + 7:sb + 8], OP.mult, OP.mult)
                S.step("vector", b_rstd, reads=stt, writes=stt)

                def b_norm(E, j=j, sb=sb):
                    E.activation(out=dst_xi(j), in_=src_tile_f(j), func=AF.Identity,
                                 scale=stats[:, sb + 7:sb + 8],
                                 bias=stats[:, sb + 8:sb + 9])
                S.step("scalar", b_norm, reads=(M("x_tm", j * 8192, 8192),) + stt,
                       writes=(M("xi", j * 8192, 8192),))

            for j in range(4):
                emit_ln(lambda jj: x_tm[:, jj, :],
                        lambda jj: xi[:, 4 * jj:4 * jj + 4, :]
                        .rearrange("p a c -> p (a c)"), j, [hx_t[j]])

            def b_h1(E):
                E.tensor_reduce(stats[0:1, 44:45], halo_x, axis=AX.X, op=OP.add)
                E.sync()
                E.tensor_scalar(stats[0:1, 44:45], stats[0:1, 44:45], 1.0 / D, None,
                                OP.mult, OP.bypass)
            S.step("vector", b_h1, reads=(M("cumW", 0, 8192),),
                   writes=(M("stats", 0, 192),), extra=[h_halo])
            S.step("scalar", lambda E: E.activation(
                out=halo_sq, in_=halo_x, func=AF.Square, accum_out=stats[0:1, 45:46]),
                reads=(M("cumW", 0, 8192),),
                writes=(M("stats", 0, 192), M("cumW", 8192, 8192)))

            def b_h2(E):
                E.tensor_scalar(stats[0:1, 45:46], stats[0:1, 45:46], 1.0 / D, None,
                                OP.mult, OP.bypass)
                E.tensor_tensor(stats[0:1, 46:47], stats[0:1, 44:45],
                                stats[0:1, 44:45], OP.mult)
                E.sync()
                E.tensor_tensor(stats[0:1, 45:46], stats[0:1, 45:46],
                                stats[0:1, 46:47], OP.subtract)
            S.step("vector", b_h2, reads=(M("stats", 0, 192),),
                   writes=(M("stats", 0, 192),))
            S.step("scalar", lambda E: E.activation(
                out=stats[0:1, 46:47], in_=stats[0:1, 45:46], func=AF.Sqrt,
                bias=consts[0:1, 0:1]),
                reads=(M("stats", 0, 192), M("consts")), writes=(M("stats", 0, 192),))

            def b_h3(E):
                E.reciprocal(stats[0:1, 47:48], stats[0:1, 46:47])
                E.sync()
                E.scalar_tensor_tensor(stats[0:1, 45:46], stats[0:1, 44:45], -1.0,
                                       stats[0:1, 47:48], OP.mult, OP.mult)
            S.step("vector", b_h3, reads=(M("stats", 0, 192),),
                   writes=(M("stats", 0, 192),))
            S.step("scalar", lambda E: E.activation(
                out=halo_x, in_=halo_x, func=AF.Identity, scale=stats[0:1, 47:48],
                bias=stats[0:1, 45:46]),
                reads=(M("stats", 0, 192), M("cumW", 0, 8192)),
                writes=(M("cumW", 0, 8192),))
            g_dma(halo_hbm[:].rearrange("(o c) -> o c", o=1), halo_x,
                  reads=(M("cumW", 0, 8192),), writes=(HT("halo"),))
            g_dma(hsc[:, 0:128], halo_hbm.rearrange("(j p) -> j p", j=16),
                  reads=(HT("halo"), M("hsc")), writes=(M("hsc"),))
            S.step("tensor", lambda E: E.transpose(PB[7][:, 0:16], hsc[:, 0:128],
                                                   idn[0:16, 0:16]),
                   reads=(M("hsc"), M("idn")), writes=(PT(7),))
            S.step("scalar", lambda E: E.activation(out=xnh[:, :], in_=PB[7][:, 0:16],
                                                    func=AF.Identity),
                   reads=(PT(7),), writes=(M("xnh"),))

            for j in range(4):
                for g4 in range(4):
                    bank = 4 + ((4 * j + g4) % 4)

                    def b_tr(E, j=j, g4=g4, bank=bank):
                        for ii in range(4):
                            i = 4 * g4 + ii
                            src = xi[:, 4 * j + i // 4, 128 * (i % 4):128 * (i % 4) + 128]
                            E.transpose(PB[bank][:, 128 * ii:128 * ii + 128], src,
                                        idn[:, :])
                    S.step("tensor", b_tr,
                           reads=(M("xi", j * 8192, 8192), M("idn")),
                           writes=(PT(bank),))

                    def b_ev(E, j=j, g4=g4, bank=bank):
                        E.activation(
                            out=xn_fm[:, 4 * g4:4 * g4 + 4, 128 * j:128 * j + 128],
                            in_=PB[bank][:, 0:512]
                            .rearrange("p (a c) -> p a c", a=4), func=AF.Identity)
                    S.step("scalar", b_ev, reads=(PT(bank),),
                           writes=(M("xn_fm", (4 * g4) * 2048, 8192),))

            if DEBUG >= 5 and b == 1:
                g_dma(dbgx_d[:, :], x_tm[:, 2, :],
                      reads=(M("x_tm", 2 * 8192, 8192),), writes=(HT("dbgx"),))
                g_dma(dbgxi_d[:, :],
                      xi[:, 8:12, :].rearrange("p a c -> p (a c)"),
                      reads=(M("xi", 8 * 2048, 4 * 2048),), writes=(HT("dbgx"),))
                g_dma(dbgxn_d[:, :],
                      xn_fm[:, :, :].rearrange("p a c -> p (a c)"),
                      reads=(M("xn_fm"),), writes=(HT("dbgx"),))

            for jt in range(16):
                dl = (delta1 if jt < 8 else delta2)[:, jt % 8, :]
                dlname = "delta1" if jt < 8 else "delta2"

                def b_dl(E, jt=jt, dl=dl):
                    E.tensor_tensor(dl[:, 1:TB], xn_fm[:, jt, 0:TB - 1],
                                    xn_fm[:, jt, 1:TB], OP.subtract)
                    E.tensor_tensor(dl[:, 0:1], xnh[:, jt:jt + 1],
                                    xn_fm[:, jt, 0:1], OP.subtract)
                S.step("gpsimd", b_dl,
                       reads=(M("xn_fm", jt * 2048, 2048), M("xnh")),
                       writes=(M(dlname, (jt % 8) * 2048, 2048),))

            for half in range(2):
                hW = sync_load(wx1b[:, :, :],
                               wx1_d[:, half * 2560:(half + 1) * 2560]
                               .rearrange("p (a c) -> p a c", a=8),
                               reads=(M("wx1b"),), writes=(M("wx1b"),))
                for dd in range(8):
                    d = 8 * half + dd

                    def b_mu(E, d=d):
                        dl = (delta1 if d < 8 else delta2)[:, d % 8, :]
                        E.scalar_tensor_tensor(xir[:, d, :], dl, mu_xf[:, d:d + 1],
                                               xn_fm[:, d, :], OP.mult, OP.add)
                    S.step("vector", b_mu,
                           reads=(M("delta1" if d < 8 else "delta2",
                                    (d % 8) * 2048, 2048),
                                  M("xn_fm", d * 2048, 2048), M("mu_xf")),
                           writes=(M("xi", d * 2048, 2048),))

                    def b_mm(E, d=d, dd=dd):
                        for cc in range(3):
                            w = 128 if cc < 2 else 64
                            E.matmul(PB[cc][0:w, :],
                                     wx1b[:, dd, 128 * cc:128 * cc + w],
                                     xir[:, d, :], start=(d == 0), stop=(d == 15))
                    S.step("tensor", b_mm,
                           reads=(M("xi", d * 2048, 2048), M("wx1b")),
                           writes=(PT(0), PT(1), PT(2)), extra=[hW])
            for cc in range(3):
                w = 128 if cc < 2 else 64
                S.step("scalar", lambda E, cc=cc, w=w: E.activation(
                    out=lr[0:w, cc, :], in_=PB[cc][0:w, :], func=AF.Tanh),
                    reads=(PT(cc),), writes=(M("lr", cc * 2048, 2048),))

            def mus_and_xi(n):
                par = 64 * (n % 2)
                jt_lr = (64 * n) // 128
                hw2 = sync_load(wx2b[par:par + 64, :], wx2_d[n, :, :],
                                reads=(M("wx2b"),), writes=(M("wx2b"),))
                for d in range(16):
                    bank = 6 + d % 2

                    def b_m(E, d=d, par=par, jt_lr=jt_lr):
                        E.matmul(PB[6 + d % 2][:, :],
                                 wx2b[par:par + 64, 128 * d:128 * d + 128],
                                 lr[par:par + 64, jt_lr, :], start=True, stop=True)
                    S.step("tensor", b_m,
                           reads=(M("wx2b"), M("lr", jt_lr * 2048, 2048)),
                           writes=(PT(bank),), extra=[hw2])

                    def b_l1(E, d=d, n=n, bank=bank):
                        dl = (delta1 if d < 8 else delta2)[:, d % 8, :]
                        E.scalar_tensor_tensor(xi[:, d, :], PB[bank][:, :],
                                               xbias[:, 16 * n + d:16 * n + d + 1],
                                               dl, OP.add, OP.mult)
                    S.step("vector", b_l1,
                           reads=(PT(bank),
                                  M("delta1" if d < 8 else "delta2",
                                    (d % 8) * 2048, 2048), M("xbias")),
                           writes=(M("xi", d * 2048, 2048),))

                    def b_l2(E, d=d):
                        E.tensor_tensor(xir[:, d, :], xi[:, d, :], xn_fm[:, d, :],
                                        OP.add)
                    S.step("gpsimd", b_l2,
                           reads=(M("xi", d * 2048, 2048), M("xn_fm", d * 2048, 2048)),
                           writes=(M("xi", d * 2048, 2048),))

            def proj_fm(wsrc_d, dst, dstname):
                for wave in range(2):
                    for d in range(16):
                        slot = d % 2
                        hws = sync_load(
                            wbuf[:, slot, :],
                            wsrc_d[:, (16 * wave + d) * 512:(16 * wave + d) * 512 + 512],
                            reads=(M("wbuf", slot * 2048, 2048),),
                            writes=(M("wbuf", slot * 2048, 2048),))

                        def b_p(E, d=d, wave=wave, slot=slot):
                            for cc in range(4):
                                E.matmul(PB[cc][:, :],
                                         wbuf[:, slot, 128 * cc:128 * cc + 128],
                                         xir[:, d, :], start=(d == 0), stop=(d == 15))
                        S.step("tensor", b_p,
                               reads=(M("xi", d * 2048, 2048),
                                      M("wbuf", slot * 2048, 2048)),
                               writes=tuple(PT(cc) for cc in range(4)), extra=[hws])
                    for cc in range(4):
                        c = 4 * wave + cc
                        S.step("scalar", lambda E, cc=cc, c=c: E.activation(
                            out=dst[:, c, :], in_=PB[cc][:, :], func=AF.Identity),
                            reads=(PT(cc),), writes=(M(dstname, c * 2048, 2048),))

            if b > 0:
                S.step("vector", lambda E: E.memset(cumW[:, :, 0:1], 0.0),
                       reads=(), writes=(M("cumW"),))
            mus_and_xi(BR["w"])
            hwa = sync_load(wwab[:, :], wwa_d[:, :], reads=(M("wwab"),),
                            writes=(M("wwab"),))
            for d in range(16):
                S.step("tensor", lambda E, d=d: E.matmul(
                    PB[4][0:64, :], wwab[:, 64 * d:64 * d + 64], xir[:, d, :],
                    start=(d == 0), stop=(d == 15)),
                    reads=(M("xi", d * 2048, 2048), M("wwab")),
                    writes=(PT(4),), extra=[hwa])
            S.step("scalar", lambda E: E.activation(out=tlow[0:64, :],
                                                    in_=PB[4][0:64, :], func=AF.Tanh),
                   reads=(PT(4),), writes=(M("tlow", 0, 2048 * 64),))
            hwb = sync_load(wwbb[0:64, :], wwb_d[:, :], reads=(M("wwbb"),),
                            writes=(M("wwbb"),))
            for c in range(8):
                S.step("tensor", lambda E, c=c: E.matmul(
                    PB[c][:, :], wwbb[0:64, 128 * c:128 * c + 128], tlow[0:64, :],
                    start=True, stop=True),
                    reads=(M("tlow", 0, 2048 * 64), M("wwbb")), writes=(PT(c),),
                    extra=[hwb])
                S.step("scalar", lambda E, c=c: E.activation(
                    out=eWb[:, :], in_=PB[c][:, :], func=AF.Exp, bias=bwf[:, c:c + 1]),
                    reads=(PT(c), M("bwf")), writes=(M("eWb"),))
                S.step("vector", lambda E, c=c: E.tensor_tensor_scan(
                    cumW[:, c, 1:TB + 1], ones[:, :], eWb[:, :], 0.0,
                    OP.mult, OP.subtract),
                    reads=(M("eWb"), M("ones")),
                    writes=(M("cumW", c * (TB + 1) * 4, (TB + 1) * 4),))

            if DEBUG >= 4 and b == 1:
                S.step("scalar", lambda E: E.activation(
                    out=sqscr[:, 0:512], in_=PB[7][:, :], func=AF.Identity),
                    reads=(PT(7),), writes=(M("sqscr"),))
                g_dma(dbgw_d[0, :, 0:512], eWb[:, :], reads=(M("eWb"),),
                      writes=(HT("dbgw"),))
                g_dma(dbgw_d[1, 0:65, 0:512], tlow[:, :], reads=(M("tlow"),),
                      writes=(HT("dbgw"),))
                g_dma(dbgw_d[2, :, 0:512], sqscr[:, 0:512], reads=(M("sqscr"),),
                      writes=(HT("dbgw"),))

            mus_and_xi(BR["k"])
            proj_fm(wk_d, k_fm, "k_fm")

            if is_main and FINE >= 1:
                mus_and_xi(BR["r"])
                proj_fm(wr_d, r_fm, "r_fm")
                mus_and_xi(BR["g"])
                hga = sync_load(wwab[:, :], wga_d[:, :], reads=(M("wwab"),),
                                writes=(M("wwab"),))
                for d in range(16):
                    S.step("tensor", lambda E, d=d: E.matmul(
                        PB[4][0:64, :], wwab[:, 64 * d:64 * d + 64], xir[:, d, :],
                        start=(d == 0), stop=(d == 15)),
                        reads=(M("xi", d * 2048, 2048), M("wwab")),
                        writes=(PT(4),), extra=[hga])
                S.step("scalar", lambda E: E.activation(
                    out=tlow[0:64, :], in_=PB[4][0:64, :], func=AF.Tanh),
                    reads=(PT(4),), writes=(M("tlow", 0, 2048 * 64),))
                hgb = sync_load(wwbb[:, :], wgb_d[:, :], reads=(M("wwbb"),),
                                writes=(M("wwbb"),))
                for tc in range(4):
                    for vh in range(2):
                        bank = 2 * tc + vh
                        S.step("tensor", lambda E, tc=tc, vh=vh, bank=bank: E.matmul(
                            PB[bank][:, :], tlow[:, 128 * tc:128 * tc + 128],
                            wwbb[:, 512 * vh:512 * vh + 512], start=True, stop=True),
                            reads=(M("tlow"), M("wwbb")), writes=(PT(bank),),
                            extra=[hgb])
                        S.step("scalar", lambda E, tc=tc, vh=vh, bank=bank:
                               E.activation(
                                   out=sg_bf[:, tc, 512 * vh:512 * vh + 512],
                                   in_=PB[bank][:, :], func=AF.Silu),
                               reads=(PT(bank),),
                               writes=(M("sg_bf", tc * 2048, 2048),))

            mus_and_xi(BR["v"])
            for d in range(16):
                slot = d % 2
                hv0 = sync_load(wbuf[:, slot, :],
                                wv_d[:, 1024 * d:1024 * d + 512],
                                reads=(M("wbuf", slot * 2048, 2048),),
                                writes=(M("wbuf", slot * 2048, 2048),))

                def b_v0(E, d=d, slot=slot):
                    for tc in range(4):
                        E.matmul(PB[tc][:, :], xir[:, d, 128 * tc:128 * tc + 128],
                                 wbuf[:, slot, :], start=(d == 0), stop=(d == 15))
                S.step("tensor", b_v0,
                       reads=(M("xi", d * 2048, 2048), M("wbuf", slot * 2048, 2048)),
                       writes=tuple(PT(t) for t in range(4)), extra=[hv0])
            for d in range(16):
                slot = d % 2
                hv1 = sync_load(wbuf[:, slot, :],
                                wv_d[:, 1024 * d + 512:1024 * d + 1024],
                                reads=(M("wbuf", slot * 2048, 2048),),
                                writes=(M("wbuf", slot * 2048, 2048),))

                def b_v1(E, d=d, slot=slot):
                    for tc in range(4):
                        E.matmul(PB[4 + tc][:, :], xir[:, d, 128 * tc:128 * tc + 128],
                                 wbuf[:, slot, :], start=(d == 0), stop=(d == 15))
                S.step("tensor", b_v1,
                       reads=(M("xi", d * 2048, 2048), M("wbuf", slot * 2048, 2048)),
                       writes=tuple(PT(4 + t) for t in range(4)), extra=[hv1])
            for tc in range(4):
                for vh in range(2):
                    S.step("scalar", lambda E, tc=tc, vh=vh: E.activation(
                        out=v_bf[:, tc, 512 * vh:512 * vh + 512],
                        in_=PB[4 * vh + tc][:, :], func=AF.Identity),
                        reads=(PT(4 * vh + tc),),
                        writes=(M("v_bf", tc * 2048, 2048),))

            S.step("vector", lambda E: E.tensor_copy(
                xnh[:, :], xn_fm[:, :, TB - 1]),
                reads=(M("xn_fm"),), writes=(M("xnh"),))

            def b_parg(E):
                E.tensor_tensor(p16[:, :, :], cumW[:, :, 16:TB + 1:16],
                                cumW[:, :, 0:TB:16], OP.subtract)
            S.step("vector", b_parg, reads=(M("cumW"),), writes=(M("p16"),))
            S.step("scalar", lambda E: E.activation(
                out=p16[:, :, :].rearrange("p a c -> p (a c)"),
                in_=p16[:, :, :].rearrange("p a c -> p (a c)"), func=AF.Exp),
                reads=(M("p16"),), writes=(M("p16"),))

            for j in range(8):
                cwj = (M("cumW", j * (TB + 1) * 4, (TB + 1) * 4),)
                anch = cumW[:, j, 0:TB:16].unsqueeze(2).broadcast_to([128, NSUB, 16])
                aAj = (M("argA", j * 2048, 2048),)
                aBj = (M("argB", j * 2048, 2048),)

                def b_argk(E, j=j, anch=anch):
                    E.tensor_tensor(argA[:, j, :].rearrange("p (a c) -> p a c", c=16),
                                    anch,
                                    cumW[:, j, 1:TB + 1]
                                    .rearrange("p (a c) -> p a c", c=16), OP.subtract)
                S.step("vector", b_argk, reads=cwj, writes=aAj)
                S.step("scalar", lambda E, j=j: E.activation(
                    out=argB[:, j, :], in_=argA[:, j, :], func=AF.Exp, scale=-1.0),
                    reads=aAj, writes=aBj)
                if is_main:
                    S.step("vector", lambda E, j=j: E.scalar_tensor_tensor(
                        rbt[:, j, :], r_fm[:, j, :], bonus_f[:, j:j + 1],
                        argB[:, j, :], OP.mult, OP.mult),
                        reads=(M("r_fm", j * 2048, 2048), aBj[0], M("bonus_sb")),
                        writes=(M("rbt", j * 1024, 1024),))
                S.step("scalar", lambda E, j=j: E.activation(
                    out=argA[:, j, :], in_=argA[:, j, :], func=AF.Exp),
                    reads=aAj, writes=aAj)

                def b_kh(E, j=j):
                    E.tensor_tensor(khat[:, j, :], k_fm[:, j, :], argA[:, j, :],
                                    OP.mult)
                    E.sync()
                    pb_ = p16[:, j, :].unsqueeze(2).broadcast_to([128, NSUB, 16])
                    E.tensor_tensor(khat[:, j, :].rearrange("p (a c) -> p a c", c=16),
                                    khat[:, j, :].rearrange("p (a c) -> p a c", c=16),
                                    pb_, OP.mult)
                S.step("vector", b_kh,
                       reads=(M("k_fm", j * 2048, 2048), aAj[0], M("p16")),
                       writes=(M("khat", j * 2048, 2048),))
                S.step("vector", lambda E, j=j: E.tensor_tensor(
                    k_bf[:, j, :], k_fm[:, j, :], argA[:, j, :], OP.mult),
                    reads=(M("k_fm", j * 2048, 2048), aAj[0]),
                    writes=(M("k_bf", j * 1024, 1024),))
                if is_main and FINE >= 2:
                    def b_argr(E, j=j, anch=anch):
                        E.tensor_tensor(argB[:, j, :]
                                        .rearrange("p (a c) -> p a c", c=16),
                                        cumW[:, j, 0:TB]
                                        .rearrange("p (a c) -> p a c", c=16),
                                        anch, OP.subtract)
                    S.step("vector", b_argr, reads=cwj, writes=aBj)
                    S.step("scalar", lambda E, j=j: E.activation(
                        out=argB[:, j, :], in_=argB[:, j, :], func=AF.Exp),
                        reads=aBj, writes=aBj)
                    S.step("vector", lambda E, j=j: E.tensor_tensor(
                        r_bf[:, j, :], r_fm[:, j, :], argB[:, j, :], OP.mult),
                        reads=(M("r_fm", j * 2048, 2048), aBj[0]),
                        writes=(M("r_bf", j * 1024, 1024),))

            if DEBUG >= 3 and b == 1:
                g_dma(dbgc_d[:, :],
                      cumW[:, :, :].rearrange("p a c -> p (a c)"),
                      reads=(M("cumW"),), writes=(HT("dbgc"),))
            do_sc = is_main and FINE >= 3
            do_cr = do_sc and FINE3 >= 2
            do_oe = do_sc and FINE3 >= 3
            if do_sc:
                S.step("vector", lambda E: E.memset(
                    ksc[:, :, :].rearrange("p a c -> p (a c)"), 0.0),
                    writes=(M("ksc"),))
                S.step("vector", lambda E: E.memset(
                    rsc[:, :, :].rearrange("p a c -> p (a c)"), 0.0),
                    writes=(M("rsc"),))
            for gg in range(4):
                par = gg % 2
                ktok = ("khtm_%d" % par,) + M("khtm2", par * 16384, 16384)
                vtok = ("v2_%d" % par,) + M("v2bf", par * 16384, 16384)
                for hh in range(2):
                    def b_kT(E, gg=gg, hh=hh):
                        for jj in range(4):
                            j = 4 * hh + jj
                            E.transpose(PB[6][:, 128 * jj:128 * jj + 128],
                                        khat[:, j, 128 * gg:128 * gg + 128],
                                        idn[:, :])
                    S.step("tensor", b_kT,
                           reads=(M("khat", 4 * hh * 2048, 4 * 2048), M("idn")),
                           writes=(PT(6),))
                    S.step("scalar", lambda E, hh=hh: E.activation(
                        out=khst[:, 512 * hh:512 * hh + 512], in_=PB[6][:, 0:512],
                        func=AF.Identity),
                        reads=(PT(6),), writes=(M("khst", hh * 1024, 1024),))
                hrm = []
                for s8 in range(8):
                    hrm.append(g_dma(
                        khtm2[:, par, s8, :], khst[16 * s8:16 * s8 + 16, :],
                        reads=(M("khst"),) + (ktok,), writes=(ktok,)))
                    hrm.append(g_dma(
                        v2bf[:, par, s8, :],
                        v_bf[16 * s8:16 * s8 + 16, gg, :],
                        reads=(M("v_bf", gg * 2048, 2048),) + (vtok,),
                        writes=(vtok,)))

                if do_sc:
                    for h4 in range(4):
                        def b_stg(E, h4=h4, gg=gg):
                            for hh in range(4):
                                h = 4 * h4 + hh
                                pr = 64 * (h % 2)
                                jt = h // 2
                                E.tensor_copy(
                                    ksc[pr:pr + 64, h, :],
                                    k_bf[pr:pr + 64, jt,
                                         128 * gg:128 * gg + 128])
                                E.tensor_copy(
                                    rsc[pr:pr + 64, h, :],
                                    r_bf[pr:pr + 64, jt,
                                         128 * gg:128 * gg + 128])
                        S.step("vector", b_stg,
                               reads=(M("k_bf"), M("r_bf"),
                                      M("ksc", h4 * 1024, 1024),
                                      M("rsc", h4 * 1024, 1024)),
                               writes=(M("ksc", h4 * 1024, 1024),
                                       M("rsc", h4 * 1024, 1024)))
                    if DEBUG >= 2 and b == 1:
                        g_dma(dbgk_d[gg, 0, :, :],
                              ksc[:, :, :].rearrange("p a c -> p (a c)"),
                              reads=(M("ksc"),), writes=(HT("dbgk"),))
                        g_dma(dbgk_d[gg, 1, :, :],
                              rsc[:, :, :].rearrange("p a c -> p (a c)"),
                              reads=(M("rsc"),), writes=(HT("dbgk"),))
                    for w in range(8):
                        bank = (0, 1, 4, 5)[w % 4]

                        def b_sc(E, w=w, gg=gg, bank=bank):
                            for hh in range(2):
                                h = 2 * w + hh
                                E.matmul(PB[bank][:, 256 * hh:256 * hh + 128],
                                         ksc[:, h, :],
                                         r_bf[:, w, 128 * gg:128 * gg + 128],
                                         start=True, stop=True)
                                E.matmul(PB[bank][:, 256 * hh + 128:
                                                  256 * hh + 256],
                                         ksc[:, h, :],
                                         rbt[:, w, 128 * gg:128 * gg + 128],
                                         start=True, stop=True)
                        S.step("tensor", b_sc,
                               reads=(M("ksc", w * 512, 512), M("r_bf"),
                                      M("rbt")),
                               writes=(PT(bank),))
                        for hh in range(2):
                            h = 2 * w + hh
                            ms = h % 4

                            def b_mk(E, bank=bank, hh=hh, ms=ms):
                                E.tensor_tensor(
                                    msc[:, ms, :],
                                    PB[bank][:, 256 * hh:256 * hh + 256],
                                    cmask[:, :], OP.mult)
                            S.step("vector", b_mk,
                                   reads=(PT(bank), M("cmask"),
                                          M("msc", ms * 512, 512)),
                                   writes=(M("msc", ms * 512, 512),))

                            def b_in(E, h=h, ms=ms, gg=gg):
                                pbi = PB[2] if h < 8 else PB[3]
                                col = 64 * (h % 8)
                                E.matmul(pbi[:, col:col + 64],
                                         msc[:, ms, 0:128],
                                         v_bf[:, gg, 64 * h:64 * h + 64],
                                         start=True, stop=False)
                                E.matmul(pbi[:, col:col + 64],
                                         msc[:, ms, 128:256],
                                         v_bf[:, gg, 64 * h:64 * h + 64],
                                         start=False, stop=True)
                            S.step("tensor", b_in,
                                   reads=(M("msc", ms * 512, 512),
                                          M("v_bf", gg * 2048, 2048)),
                                   writes=(PT(2 if h < 8 else 3),))

                for ii in range(8):
                    i = 8 * gg + ii
                    rb = 64 * (ii % 2)

                    def b_cs(E, ii=ii, par=par, rb=rb, do_cr=do_cr):
                        if do_cr:
                            for h in range(16):
                                jt = h // 2
                                pb_o = PB[4] if h < 8 else PB[5]
                                col = 64 * (h % 8)
                                E.matmul(pb_o[rb:rb + 16, col:col + 64],
                                         rsc[:, h, 16 * ii:16 * ii + 16],
                                         S_bf[:, 64 * jt:64 * jt + 64],
                                         start=True, stop=True)
                        for h in range(16):
                            pr = 64 * (h % 2)
                            E.matmul(PB[1][pr:pr + 64,
                                           64 * (h // 2):64 * (h // 2) + 64],
                                     khtm2[:, par, ii, 64 * h:64 * h + 64],
                                     v2bf[:, par, ii, 64 * h:64 * h + 64],
                                     start=True, stop=True)
                    if do_cr:
                        S.step("tensor", b_cs,
                               reads=(M("rsc"), M("S_bf"), ktok, vtok),
                               writes=(PT(1), PT(4), PT(5)), extra=hrm)
                    else:
                        S.step("tensor", b_cs, reads=(ktok, vtok),
                               writes=(PT(1),), extra=hrm)

                    def b_bl(E, i=i):
                        for hg in range(8):
                            E.scalar_tensor_tensor(
                                S_sb[:, 64 * hg:64 * hg + 64],
                                S_sb[:, 64 * hg:64 * hg + 64],
                                p16[:, hg, i:i + 1],
                                PB[1][:, 64 * hg:64 * hg + 64], OP.mult, OP.add)
                    S.step("vector", b_bl, reads=(PT(1), M("S_sb"), M("p16")),
                           writes=(M("S_sb"),))
                    if do_cr or (gg == 3 and ii == 7):
                        S.step("scalar", lambda E: E.activation(
                            out=S_bf[:, :], in_=S_sb[:, :], func=AF.Identity),
                            reads=(M("S_sb"),), writes=(M("S_bf"),))

                    if do_oe:
                        def b_oe(E, ii=ii, rb=rb):
                            E.activation(out=ostage[:, ii % 2, 0:512],
                                         in_=PB[4][rb:rb + 16, :],
                                         func=AF.Identity)
                            E.activation(out=ostage[:, ii % 2, 512:1024],
                                         in_=PB[5][rb:rb + 16, :],
                                         func=AF.Identity)
                        S.step("scalar", b_oe,
                               reads=(PT(4), PT(5),
                                      M("ostage", (ii % 2) * 4096, 4096)),
                               writes=(M("ostage", (ii % 2) * 4096, 4096),))
                        g_dma(o_tm[16 * ii:16 * ii + 16, gg, :],
                              ostage[:, ii % 2, :],
                              reads=(M("ostage", (ii % 2) * 4096, 4096),),
                              writes=(M("o_tm", gg * 4096, 4096),))

                if DEBUG >= 2 and b == 1:
                    g_dma(dbgs_d[gg, :, :], S_sb[:, :],
                          reads=(M("S_sb"),), writes=(HT("dbgs"),))

                if do_oe:
                    for half in range(2):
                        def b_oa(E, gg=gg, half=half):
                            E.tensor_tensor(
                                o_tm[:, gg, 512 * half:512 * half + 512],
                                o_tm[:, gg, 512 * half:512 * half + 512],
                                PB[2 + half][:, :], OP.add)
                        S.step("vector", b_oa,
                               reads=(PT(2 + half),
                                      M("o_tm", gg * 4096 + half * 2048, 2048)),
                               writes=(M("o_tm", gg * 4096 + half * 2048,
                                          2048),))

            if not is_main:
                continue

            if DEBUG:
                for gg in range(4):
                    g_dma(odbg_d[b - 1, :, 1024 * gg:1024 * gg + 1024],
                          o_tm[:, gg, :],
                          reads=(M("o_tm", gg * 4096, 4096),),
                          writes=(HT("odbg"),))

            if FINE < 4:
                continue
            for tc in range(4):
                otok = (M("o_tm", tc * 4096, 4096),)
                S.step("vector", lambda E, tc=tc: E.tensor_reduce(
                    gnst[:, 0:16],
                    o_tm[:, tc, :].rearrange("p (h v) -> p h v", v=64),
                    axis=AX.X, op=OP.add),
                    reads=otok, writes=(M("gnst"),))
                S.step("scalar", lambda E, tc=tc: E.activation(
                    out=sqscr[:, :], in_=o_tm[:, tc, :], func=AF.Square),
                    reads=otok, writes=(M("sqscr"),))

                def b_ge2(E):
                    E.tensor_reduce(gnst[:, 16:32],
                                    sqscr[:, :].rearrange("p (h v) -> p h v", v=64),
                                    axis=AX.X, op=OP.add)
                    E.tensor_scalar(gnst[:, 0:16], gnst[:, 0:16], 1.0 / 64, None,
                                    OP.mult, OP.bypass)
                    E.sync()
                    E.tensor_scalar(gnst[:, 16:32], gnst[:, 16:32], 1.0 / 64, None,
                                    OP.mult, OP.bypass)
                    E.tensor_tensor(gnst[:, 32:48], gnst[:, 0:16], gnst[:, 0:16],
                                    OP.mult)
                    E.sync()
                    E.tensor_tensor(gnst[:, 16:32], gnst[:, 16:32], gnst[:, 32:48],
                                    OP.subtract)
                S.step("vector", b_ge2, reads=(M("sqscr"), M("gnst")),
                       writes=(M("gnst"),))
                S.step("scalar", lambda E: E.activation(
                    out=gnst[:, 32:48], in_=gnst[:, 16:32], func=AF.Sqrt,
                    bias=consts[:, 0:1]),
                    reads=(M("gnst"), M("consts")), writes=(M("gnst"),))

                def b_grs(E):
                    E.reciprocal(gnst[:, 48:64], gnst[:, 32:48])
                    E.sync()
                    E.scalar_tensor_tensor(gnst[:, 64:80], gnst[:, 0:16], -1.0,
                                           gnst[:, 48:64], OP.mult, OP.mult)
                S.step("vector", b_grs, reads=(M("gnst"),), writes=(M("gnst"),))

                def b_gn(E, tc=tc):
                    ov = o_tm[:, tc, :].rearrange("p (h v) -> p h v", v=64)
                    rs = gnst[:, 48:64].unsqueeze(2).broadcast_to([128, 16, 64])
                    nm = gnst[:, 64:80].unsqueeze(2).broadcast_to([128, 16, 64])
                    E.tensor_tensor(ov, ov, rs, OP.mult)
                    E.sync()
                    E.tensor_tensor(ov, ov, nm, OP.add)
                    E.sync()
                    E.tensor_tensor(o_tm[:, tc, :], o_tm[:, tc, :], lnw[:, :],
                                    OP.mult)
                    E.sync()
                    E.tensor_tensor(o_tm[:, tc, :], o_tm[:, tc, :], lnb[:, :],
                                    OP.add)
                    E.sync()
                    E.tensor_tensor(o_tm[:, tc, :], o_tm[:, tc, :], sg_bf[:, tc, :],
                                    OP.mult)
                S.step("vector", b_gn,
                       reads=(M("gnst"), M("lnw"), M("lnb"),
                              M("sg_bf", tc * 2048, 2048)) + otok,
                       writes=otok)

            if FINE < 5:
                continue
            for tc in range(4):
                for g2 in range(2):
                    bank = (2 * tc + g2) % 4 + 2

                    def b_ot(E, tc=tc, g2=g2, bank=bank):
                        for jj in range(4):
                            j = 4 * g2 + jj
                            E.transpose(PB[bank][:, 128 * jj:128 * jj + 128],
                                        o_tm[:, tc, 128 * j:128 * j + 128],
                                        idn[:, :])
                    S.step("tensor", b_ot,
                           reads=(M("o_tm", tc * 4096, 4096), M("idn")),
                           writes=(PT(bank),))
                    S.step("scalar", lambda E, tc=tc, g2=g2, bank=bank: E.activation(
                        out=o2_fm[:, 4 * g2:4 * g2 + 4, 128 * tc:128 * tc + 128],
                        in_=PB[bank][:, 0:512]
                        .rearrange("p (a c) -> p a c", a=4), func=AF.Identity),
                        reads=(PT(bank),),
                        writes=(M("o2_fm", 4 * g2 * 2048, 8192),))

            hx2 = []
            for j in range(4):
                hx2.append(g_dma(
                    x_tm[:, j, :], xs_d[lo + 1 + j * 128:lo + 1 + (j + 1) * 128, :],
                    reads=(M("x_tm", j * 8192, 8192),),
                    writes=(M("x_tm", j * 8192, 8192),)))
            for tp in range(2):
                for vc in range(8):
                    slot = vc % 2
                    hwo = sync_load(wobuf[:, slot, :],
                                    wo_d[:, 2048 * vc:2048 * (vc + 1)],
                                    reads=(M("wobuf", slot * 8192, 8192),),
                                    writes=(M("wobuf", slot * 8192, 8192),))

                    def b_wo(E, tp=tp, vc=vc, slot=slot):
                        for ti in range(2):
                            tc = 2 * tp + ti
                            for dg in range(4):
                                E.matmul(PB[4 * ti + dg][:, :],
                                         o2_fm[:, vc, 128 * tc:128 * tc + 128],
                                         wobuf[:, slot, 512 * dg:512 * dg + 512],
                                         start=(vc == 0), stop=(vc == 7))
                    S.step("tensor", b_wo,
                           reads=(M("o2_fm", vc * 2048, 2048),
                                  M("wobuf", slot * 8192, 8192)),
                           writes=tuple(PT(p) for p in range(8)),
                           extra=[hwo])
                for ti in range(2):
                    tc = 2 * tp + ti
                    for dg in range(4):
                        S.step("vector", lambda E, tc=tc, dg=dg, ti=ti: E.tensor_tensor(
                            x_tm[:, tc, 512 * dg:512 * dg + 512],
                            x_tm[:, tc, 512 * dg:512 * dg + 512],
                            PB[4 * ti + dg][:, :], OP.add),
                            reads=(PT(4 * ti + dg),
                                   M("x_tm", tc * 8192 + dg * 2048, 2048)),
                            writes=(M("x_tm", tc * 8192 + dg * 2048, 2048),),
                            extra=[hx2[tc]])

            if FINE < 6:
                continue
            for j in range(4):
                emit_ln(lambda jj: x_tm[:, jj, :],
                        lambda jj: xi[:, 4 * jj:4 * jj + 4, :]
                        .rearrange("p a c -> p (a c)"), j, [])
                g_dma(x1_hbm[TB * (b - 1) + 128 * j:TB * (b - 1) + 128 * (j + 1), :],
                      x_tm[:, j, :], reads=(M("x_tm", j * 8192, 8192),),
                      writes=(HT("x1"),))
            for j in range(4):
                for g4 in range(4):
                    bank = 4 + ((4 * j + g4) % 4)

                    def b_tr2(E, j=j, g4=g4, bank=bank):
                        for ii in range(4):
                            i = 4 * g4 + ii
                            src = xi[:, 4 * j + i // 4,
                                     128 * (i % 4):128 * (i % 4) + 128]
                            E.transpose(PB[bank][:, 128 * ii:128 * ii + 128], src,
                                        idn[:, :])
                    S.step("tensor", b_tr2,
                           reads=(M("xi", 4 * j * 8192 // 4, 8192), M("idn")),
                           writes=(PT(bank),))
                    S.step("scalar", lambda E, j=j, g4=g4, bank=bank: E.activation(
                        out=xn2f[:, 4 * g4:4 * g4 + 4, 128 * j:128 * j + 128],
                        in_=PB[bank][:, 0:512]
                        .rearrange("p (a c) -> p a c", a=4), func=AF.Identity),
                        reads=(PT(bank),),
                        writes=(M("xn_fm", 4 * g4 * 2048, 8192),))
            g_dma(xn2_hbm[:, (b - 1) * 16 * TB:b * 16 * TB],
                  xn2f[:, :, :].rearrange("p a c -> p (a c)"),
                  reads=(M("xn_fm"),), writes=(HT("xn2"),))

        for hf in range(2 if TRUNC >= 5 else 0):
            g_dma(xn2sb[:, :, hf * TB:(hf + 1) * TB],
                  xn2_hbm[:, hf * 16 * TB:(hf + 1) * 16 * TB]
                  .rearrange("p (a c) -> p a c", a=16),
                  reads=(HT("xn2"), M("xn2sb")), writes=(M("xn2sb"),))

        for s in range(NSLAB if TRUNC >= 5 else 0):
            par = s % 2
            hg = sync_load(wgbuf[:, par, :], wgf_d[:, 2048 * s:2048 * (s + 1)],
                           reads=(M("wgbuf", par * 8192, 8192),),
                           writes=(M("wgbuf", par * 8192, 8192),))
            hu = sync_load(wubuf[:, par, :], wuf_d[:, 2048 * s:2048 * (s + 1)],
                           reads=(M("wubuf", par * 8192, 8192),),
                           writes=(M("wubuf", par * 8192, 8192),))
            pbg = (0, 1) if par == 0 else (4, 5)
            pbu = (2, 3) if par == 0 else (6, 7)
            for th in range(2):
                def b_g(E, s=s, th=th, par=par, pbg=pbg):
                    for d in range(16):
                        E.matmul(PB[pbg[th]][:, :],
                                 wgbuf[:, par, 128 * d:128 * d + 128],
                                 xn2sb[:, d, 512 * th:512 * th + 512],
                                 start=(d == 0), stop=(d == 15))
                S.step("tensor", b_g, reads=(M("wgbuf", par * 8192, 8192),
                                             M("xn2sb")),
                       writes=(PT(pbg[th]),), extra=[hg])

                def b_u(E, s=s, th=th, par=par, pbu=pbu):
                    for d in range(16):
                        E.matmul(PB[pbu[th]][:, :],
                                 wubuf[:, par, 128 * d:128 * d + 128],
                                 xn2sb[:, d, 512 * th:512 * th + 512],
                                 start=(d == 0), stop=(d == 15))
                S.step("tensor", b_u, reads=(M("wubuf", par * 8192, 8192),
                                             M("xn2sb")),
                       writes=(PT(pbu[th]),), extra=[hu])
                S.step("scalar", lambda E, th=th, pbg=pbg: E.activation(
                    out=sqscr[:, 512 * th:512 * th + 512], in_=PB[pbg[th]][:, :],
                    func=AF.Silu),
                    reads=(PT(pbg[th]),),
                    writes=(M("sqscr", th * 2048, 2048),))
                S.step("vector", lambda E, th=th, par=par, pbu=pbu: E.tensor_tensor(
                    hstage[:, par, 512 * th:512 * th + 512],
                    sqscr[:, 512 * th:512 * th + 512], PB[pbu[th]][:, :], OP.mult),
                    reads=(M("sqscr", th * 2048, 2048), PT(pbu[th]),
                           M("hstage", par * 2048, 2048)),
                    writes=(M("hstage", par * 2048, 2048),))
            g_dma(h_hbm[:, 1024 * s:1024 * (s + 1)], hstage[:, par, :],
                  reads=(M("hstage", par * 2048, 2048),), writes=(HT("h"),))

        for q4 in range(4 if TRUNC >= 5 else 0):
            n = 11 if q4 < 3 else 10
            g_dma(hbuf[:, 11 * q4:11 * q4 + n, :]
                  .rearrange("p a c -> p (a c)"),
                  h_hbm[:, 11 * q4 * 1024:(11 * q4 + n) * 1024],
                  reads=(HT("h"), M("hbuf")), writes=(M("hbuf"),))

        for dgrp in range(4 if TRUNC >= 5 else 0):
            hx1 = []
            for tc in range(8):
                hx1.append(g_dma(
                    x1st[:, tc, :],
                    x1_hbm[128 * tc:128 * (tc + 1), 512 * dgrp:512 * dgrp + 512],
                    reads=(HT("x1"), M("x1st", tc * 2048, 2048)),
                    writes=(M("x1st", tc * 2048, 2048),)))
            for oc in range(NSLAB):
                slot = oc % 2
                hd = sync_load(wdbuf[:, slot, :],
                               wdf_d[:, 2048 * oc + 512 * dgrp:
                                     2048 * oc + 512 * dgrp + 512],
                               reads=(M("wdbuf", slot * 2048, 2048),),
                               writes=(M("wdbuf", slot * 2048, 2048),))
                S.step("scalar", lambda E, slot=slot: E.activation(
                    out=wdbf[:, slot, :], in_=wdbuf[:, slot, :], func=AF.Identity),
                    reads=(M("wdbuf", slot * 2048, 2048),
                           M("wdbf", slot * 1024, 1024)),
                    writes=(M("wdbf", slot * 1024, 1024),), extra=[hd])

                def b_d(E, oc=oc, slot=slot):
                    for tc in range(8):
                        E.matmul(PB[tc][:, :], hbuf[:, oc, 128 * tc:128 * tc + 128],
                                 wdbf[:, slot, :], start=(oc == 0),
                                 stop=(oc == NSLAB - 1))
                S.step("tensor", b_d,
                       reads=(M("hbuf", oc * 2048, 2048),
                              M("wdbf", slot * 1024, 1024)),
                       writes=tuple(PT(t) for t in range(8)))
            for tc in range(8):
                par = tc % 2
                S.step("vector", lambda E, tc=tc, par=par: E.tensor_tensor(
                    ystage[:, par, :], x1st[:, tc, :], PB[tc][:, :], OP.add),
                    reads=(PT(tc), M("x1st", tc * 2048, 2048),
                           M("ystage", par * 2048, 2048)),
                    writes=(M("ystage", par * 2048, 2048),), extra=[hx1[tc]])
                g_dma(y_d[128 * tc:128 * (tc + 1), 512 * dgrp:512 * dgrp + 512],
                      ystage[:, par, :],
                      reads=(M("ystage", par * 2048, 2048),), writes=(HT("y"),))

        fin_total = [0]

        def b_fin(E):
            E.wait_ge(S.semobj["gq"], fin_total[0])
        S.step("gpsimd", b_fin, reads=(HT("y"),))
        fin_total[0] = S.dma_cum["gq"]

        with nc.Block() as block:
            for e in ("sync", "gpsimd", "tensor", "vector", "scalar"):
                def make(e):
                    def runner(E):
                        for fn in S.q[e]:
                            fn(E)
                    return runner
                getattr(block, e)(make(e))

    return nc, S.dma_meas


def kernel(**inputs):
    H = _host_prep(inputs)
    if "nc" not in _CACHE:
        _, meas = build_program()
        nc, _ = build_program(dma_counts=meas)
        _CACHE["nc"] = nc
    nc = _CACHE["nc"]
    shared = {k: v for k, v in H.items() if k != "xs"}
    in_maps = []
    for c in range(8):
        m = dict(shared)
        m["xs"] = H["xs"][c]
        in_maps.append(m)
    res = run_bass_kernel_spmd(nc, in_maps, core_ids=list(range(8)))
    out = np.empty((2, 4096, D), dtype=np.float32)
    for c in range(8):
        b, q = c // 4, c % 4
        out[b, q * 1024:(q + 1) * 1024] = res.results[c]["y"]
    return out
